# version 04 (final)
# baseline (speedup 1.0000x reference)
import numpy as np
import ml_dtypes
from contextlib import ExitStack

import concourse.bass as bass
import concourse.mybir as mybir
from concourse.bass_utils import run_bass_kernel_spmd

F32 = mybir.dt.float32
BF16 = mybir.dt.bfloat16
I8 = mybir.dt.int8
AF = mybir.ActivationFunctionType
ALU = mybir.AluOpType
AX = mybir.AxisListType

D = 2048
NDC = 16
NT = 1024
HID = 8192
NHC = 64
NPAIR = 16
EPS = 1e-5
QB = 127.0
N_GV = D * 16384
N_OUT = D * HID
MAGIC = 12582912.0
DBG = False
FQ = 256

ENGS = ("sync", "gpsimd", "scalar", "tensor", "vector")
CHANNELS = ("cW", "cS", "cG", "cCC", "cT", "cV", "cSc")


class Sched:

    def __init__(self):
        self.ops = {e: [] for e in ENGS}
        self.emitted = {e: 0 for e in ENGS}
        self.waited = {e: {} for e in ENGS}
        self.cnt = {c: 0 for c in CHANNELS}

    def add(self, eng, emit, waits=(), inc=None, k=1):
        val = None
        if inc is not None:
            self.cnt[inc] += k
            val = self.cnt[inc]
        self.ops[eng].append((emit, tuple(waits), inc, k))
        return val

    def ch(self, name):
        self.cnt.setdefault(name, 0)
        return name

    def dma(self, eng, out, in_, waits=(), ch=None):
        q = ch or {"sync": "cW", "scalar": "cS", "gpsimd": "cG"}[eng]
        self.ch(q)

        def emit(e, out=out, in_=in_):
            return e.dma_start(out=out, in_=in_)

        return self.add(eng, emit, waits, inc=q, k=16)

    def emit_partial(self, block, sems):
        for eng in ENGS:
            ops = self.ops[eng][self.emitted[eng]:]
            if not ops:
                continue
            self.emitted[eng] = len(self.ops[eng])
            waited = self.waited[eng]

            def body(e, ops=ops, waited=waited):
                for emit, waits, inc, k in ops:
                    for chn, v in waits:
                        if v is None or waited.get(chn, 0) >= v:
                            continue
                        e.wait_ge(sems[chn], v)
                        waited[chn] = v
                    bi = emit(e)
                    if inc is not None:
                        if bi is None:
                            bi = e.nop()
                        bi.then_inc(sems[inc], k)

            getattr(block, eng)(body)


def build_program():
    nc = bass.Bass("TRN2", target_bir_lowering=False,
                   detect_race_conditions=False)

    xT = nc.dram_tensor("xT", [D, NT], F32, kind="ExternalInput").ap()
    wq_d = nc.dram_tensor("wq", [D, D], F32, kind="ExternalInput").ap()
    wk_d = nc.dram_tensor("wk", [D, D], F32, kind="ExternalInput").ap()
    wv_d = nc.dram_tensor("wv", [D, D], F32, kind="ExternalInput").ap()
    wo_d = nc.dram_tensor("wo", [D, D], F32, kind="ExternalInput").ap()
    bo_t = nc.dram_tensor("bo_t", [128, NDC], F32, kind="ExternalInput").ap()
    wn_t = nc.dram_tensor("wn_t", [128, NDC], F32, kind="ExternalInput").ap()
    wgvT = nc.dram_tensor("wgvT", [D, 16384], F32, kind="ExternalInput").ap()
    woutT = nc.dram_tensor("woutT", [HID, D], F32, kind="ExternalInput").ap()
    gv_sh = nc.dram_tensor("gv_sh", [D, 2048], F32, kind="ExternalInput").ap()
    out_sh = nc.dram_tensor("out_sh", [1024, D], F32, kind="ExternalInput").ap()
    yT = nc.dram_tensor("yT", [D, NT], F32, kind="ExternalOutput").ap()

    ik = dict(kind="ExternalOutput") if DBG else {}
    u_sp = nc.dram_tensor("u_sp", [NHC, 128, NT], F32, **ik).ap()
    t2_sp = nc.dram_tensor("t2_sp", [NHC, 128, D], BF16, **ik).ap()
    aq_sp = nc.dram_tensor("aq_sp", [128, NDC * NT], I8, **ik).ap()
    ccA_in = nc.dram_tensor("ccA_in", [1, 2], F32).ap()
    ccA_out = nc.dram_tensor("ccA_out", [1, 2], F32).ap()
    ccB_in = nc.dram_tensor("ccB_in", [128, 2080], F32).ap()
    ccB_out = nc.dram_tensor("ccB_out", [128, 2080], F32).ap()
    dbg_st = (nc.dram_tensor("dbg_st", [128, 184], F32, kind="ExternalOutput")
              .ap() if DBG else None)
    dbg_kv = (nc.dram_tensor("dbg_kv", [128, 2080], F32,
                             kind="ExternalOutput").ap() if DBG else None)
    dbg_gv = (nc.dram_tensor("dbg_gv", [128, 3 * NT], F32,
                             kind="ExternalOutput").ap() if DBG else None)
    dbg_rn = (nc.dram_tensor("dbg_rn", [32, 4 * FQ], F32,
                             kind="ExternalOutput").ap() if DBG else None)
    dbg_o = (nc.dram_tensor("dbg_o", [128, 2 * FQ], F32,
                            kind="ExternalOutput").ap() if DBG else None)
    dbg_ksd = (nc.dram_tensor("dbg_ksd", [128, 2 * NPAIR * 32], BF16,
                              kind="ExternalOutput").ap() if DBG else None)
    dbg_ksin = (nc.dram_tensor("dbg_ksin", [128, 16], F32,
                               kind="ExternalOutput").ap() if DBG else None)

    ident_d = nc.inline_tensor(np.eye(128, dtype=np.float32), "ident_d").ap()
    identb_d = nc.inline_tensor(np.eye(128, dtype=ml_dtypes.bfloat16),
                                "identb_d").ap()
    ones_d = nc.inline_tensor(np.ones((128, 128), np.float32), "ones_d").ap()
    ed = np.zeros((32, NPAIR * 128), dtype=ml_dtypes.bfloat16)
    for j in range(NPAIR):
        ed[2 * j, j * 128:j * 128 + 64] = 1
        ed[2 * j + 1, j * 128 + 64:(j + 1) * 128] = 1
    ediag_d = nc.inline_tensor(ed, "ediag_d").ap()

    S = Sched()
    ctx = ExitStack()
    E = ctx.enter_context
    with ctx:
        sems = {}
        S.ch("cVB")
        _vbn = {"n": 0}

        def VB(e, bi):
            _vbn["n"] += 1
            e.nop().then_inc(sems["cVB"], 1)
            e.wait_ge(sems["cVB"], _vbn["n"])
            return bi

        def VBR(e, bi):
            _vbn["n"] += 1
            e.nop().then_inc(sems["cVB"], 1)
            e.wait_ge(sems["cVB"], _vbn["n"])
            return bi

        def _sync_sems():
            for c in S.cnt:
                if c not in sems:
                    sems[c] = E(nc.semaphore(c))

        block = E(nc.Block())

        s16 = E(nc.sbuf_tensor("s16", [128, NDC * NT], F32))
        ident = E(nc.sbuf_tensor("ident", [128, 128], F32))
        identb = E(nc.sbuf_tensor("identb", [128, 128], BF16))
        ones = E(nc.sbuf_tensor("ones", [128, 128], F32))
        ediag = E(nc.sbuf_tensor("ediag", [32, NPAIR * 128], BF16))
        bo_s = E(nc.sbuf_tensor("bo_s", [128, NDC], F32))
        wn_s = E(nc.sbuf_tensor("wn_s", [128, NDC], F32))
        st = E(nc.sbuf_tensor("st", [128, 176], F32))
        COL = {}
        for i, name in enumerate(
            ["sum", "sumsq", "m", "ex2", "var", "vpe", "rstd", "rms", "m1",
             "mxa", "desca", "sca", "rs", "cda", "tmp1", "tmp2", "mxu",
             "desu", "su", "c2", "mr", "s0"]):
            COL[name] = st[:, i * 8:(i + 1) * 8]
        sw2 = E(nc.sbuf_tensor("sw2", [128, 4], F32))
        rowt = E(nc.sbuf_tensor("rowt", [1, 4], F32))
        ks_sb = E(nc.sbuf_tensor("ks_sb", [128, 16], F32))
        rtb = [E(nc.sbuf_tensor(f"rtb{i}", [128, 128], F32)) for i in range(2)]
        Bb1 = E(nc.sbuf_tensor("Bb1", [128, NT], F32))
        Bb2 = E(nc.sbuf_tensor("Bb2", [128, NT], F32))

        PBIG = E(nc.psum_tensor("PBIG", [128, 2048], F32))
        PA = E(nc.psum_tensor("PA", [128, 512], F32))
        PB = E(nc.psum_tensor("PB", [128, 512], F32))
        PC = E(nc.psum_tensor("PC", [128, 512], F32))
        PD = E(nc.psum_tensor("PD", [128, 512], F32))
        PS4 = [PA, PB, PC, PD]

        def bcast(stat_ap, dest):
            tids, last = [], None
            for tc in range(8):
                w = [("cT", tids[tc - 2])] if tc >= 2 else []
                vts = S.add("vector",
                            (lambda e, tc=tc: VB(e, e.tensor_scalar(
                                rtb[tc % 2][:], ones[:],
                                stat_ap[:, tc:tc + 1], None, ALU.mult))),
                            w, inc="cV")
                ps = PS4[2 + tc % 2]
                tid = S.add("tensor",
                            (lambda e, tc=tc, ps=ps: e.transpose(
                                ps[:, 0:128], rtb[tc % 2][:], ident[:])),
                            [("cV", vts)], inc="cT")
                tids.append(tid)
                last = S.add("vector",
                             (lambda e, tc=tc, ps=ps: VB(e, e.tensor_copy(
                                 dest[:, tc * 128:(tc + 1) * 128],
                                 ps[:, 0:128]))),
                             [("cT", tid)], inc="cV")
            return last

        wctx = ExitStack()
        wraw = [wctx.enter_context(nc.sbuf_tensor(f"wraw{i}", [128, 2048],
                                                  F32)) for i in range(2)]
        wsph = [wctx.enter_context(nc.sbuf_tensor(f"wsph{i}", [128, 2048],
                                                  BF16)) for i in range(2)]
        wspl = [wctx.enter_context(nc.sbuf_tensor(f"wspl{i}", [128, 2048],
                                                  BF16)) for i in range(2)]
        xhctx = ExitStack()
        xh = xhctx.enter_context(nc.sbuf_tensor("xh", [128, NDC * NT], BF16))
        xl = xhctx.enter_context(nc.sbuf_tensor("xl", [128, NDC * NT], BF16))

        S.dma("sync", ident[:], ident_d[:])
        S.dma("sync", identb[:], identb_d[:])
        S.dma("sync", ones[:], ones_d[:])
        S.dma("sync", ediag[:], ediag_d[:])
        S.dma("sync", bo_s[:], bo_t[:])
        S.dma("sync", wn_s[:], wn_t[:])
        x_ld = S.dma("sync", s16[:].rearrange("p (c n) -> p c n", c=NDC),
                     xT.rearrange("(c p) n -> p c n", p=128))

        S.add("scalar", lambda e: e.preload_act_table(AF.Sqrt))


        sqctx = ExitStack()
        sqt = [sqctx.enter_context(
            nc.sbuf_tensor(f"sqt{i}", [128, NT], F32, side="right"))
            for i in range(2)]
        scrA = sqctx.enter_context(
            nc.sbuf_tensor("scrA", [128, NT], F32, side="right"))
        scrB = sqctx.enter_context(
            nc.sbuf_tensor("scrB", [128, NT], F32, side="right"))

        ps_rdy = S.add("vector", lambda e: e.memset(PA[:, 0:16], 0.0),
                       inc="cV")
        sq_ids, st_ids = [], []
        for dc in range(NDC):
            w = [("cW", x_ld)]
            if dc >= 2:
                w.append(("cT", st_ids[dc - 2]))
            sq_ids.append(S.add(
                "scalar",
                (lambda e, dc=dc: e.activation(
                    sqt[dc % 2][:], s16[:, dc * NT:(dc + 1) * NT],
                    AF.Square)),
                w, inc="cSc"))

            def stmm(e, dc=dc):
                bi = None
                for tc in range(8):
                    e.matmul(PA[:, tc:tc + 1],
                             s16[:, dc * NT + tc * 128:
                                 dc * NT + (tc + 1) * 128],
                             ones[:, 0:1], start=(dc == 0 and tc == 0),
                             stop=(dc == NDC - 1), skip_group_check=True)
                for tc in range(8):
                    bi = e.matmul(PA[:, 8 + tc:9 + tc],
                                  sqt[dc % 2][:, tc * 128:(tc + 1) * 128],
                                  ones[:, 0:1], start=False,
                                  stop=(dc == NDC - 1),
                                  skip_group_check=True)
                return bi
            st_ids.append(S.add(
                "tensor", stmm,
                [("cV", ps_rdy), ("cSc", sq_ids[dc]), ("cW", x_ld)],
                inc="cT"))

        shctx = ExitStack()
        share_b = [shctx.enter_context(
            nc.sbuf_tensor(f"share{i}", [128, 512], F32, side="right"))
            for i in range(2)]
        share_acc = shctx.enter_context(
            nc.sbuf_tensor("share_acc", [128, 96], F32, side="right"))
        sh_junk = shctx.enter_context(
            nc.sbuf_tensor("sh_junk", [128, 512], F32, side="right"))
        shr2 = shctx.enter_context(
            nc.sbuf_tensor("shr2", [128, 2], F32, side="right"))

        abs_ids = []
        for i in range(96):
            if i < 64:
                src_ap = gv_sh.rearrange("(j p) n -> p j n", p=128)[
                    :, i // 4, (i % 4) * 512:(i % 4 + 1) * 512]
            else:
                j = i - 64
                src_ap = out_sh.rearrange("(j p) n -> p j n", p=128)[
                    :, j // 4, (j % 4) * 512:(j % 4 + 1) * 512]
            dw = [("cSc", abs_ids[i - 2])] if i >= 2 else []
            did = S.dma("gpsimd", share_b[i % 2][:], src_ap, waits=dw,
                        ch=f"cSH{i % 2}")
            abs_ids.append(S.add(
                "scalar",
                (lambda e, i=i: e.activation(
                    sh_junk[:], share_b[i % 2][:], AF.Abs,
                    accum_out=share_acc[:, i:i + 1])),
                [(f"cSH{i % 2}", did)], inc="cSc"))

        vv = S.add("vector",
                   lambda e: (e.reduce_sum(shr2[:, 0:1], share_acc[:, 0:64],
                                           axis=AX.X),
                              e.reduce_sum(shr2[:, 1:2], share_acc[:, 64:96],
                                           axis=AX.X))[-1],
                   [("cSc", abs_ids[-1])], inc="cV")
        shmm = S.add("tensor",
                     lambda e: e.matmul(PB[0:1, 0:2], ones[:, 0:1], shr2[:],
                                        start=True, stop=True),
                     [("cV", vv)], inc="cT")
        vv = S.add("vector",
                   lambda e: e.tensor_copy(rowt[0:1, 0:2], PB[0:1, 0:2]),
                   [("cT", shmm)], inc="cV")
        gd = S.dma("gpsimd", ccA_in[:], rowt[0:1, 0:2], waits=[("cV", vv)])
        cc = S.add("gpsimd",
                   lambda e: e.collective_compute(
                       "AllReduce", ALU.add,
                       replica_groups=[[0, 1, 2, 3, 4, 5, 6, 7]],
                       ins=[ccA_in[:].opt()], outs=[ccA_out[:].opt()]),
                   [("cG", gd)], inc="cCC")
        gd = S.dma("gpsimd", rowt[0:1, 2:4], ccA_out[:], waits=[("cCC", cc)])

        def _swmath(e):
            VB(e, e.tensor_scalar(rowt[0:1, 2:3], rowt[0:1, 2:3], 1.0 / N_GV,
                                  EPS, ALU.mult, ALU.max))
            VB(e, e.tensor_scalar(rowt[0:1, 3:4], rowt[0:1, 3:4], 1.0 / N_OUT,
                                  EPS, ALU.mult, ALU.max))
            return VBR(e, e.reciprocal(rowt[0:1, 0:2], rowt[0:1, 2:4]))
        vv = S.add("vector", _swmath, [("cG", gd)], inc="cV")
        tv = S.add("tensor",
                   lambda e: e.matmul(PB[:, 0:4], ones[0:1, :],
                                      rowt[0:1, 0:4], start=True, stop=True),
                   [("cV", vv)], inc="cT")
        sw_done = S.add("vector",
                        lambda e: e.tensor_copy(sw2[:], PB[:, 0:4]),
                        [("cT", tv)], inc="cV")

        def _statmath(e):
            VB(e, e.tensor_copy(COL["sum"], PA[:, 0:8]))
            VB(e, e.tensor_copy(COL["sumsq"], PA[:, 8:16]))
            VB(e, e.tensor_scalar_mul(COL["m"], COL["sum"], 1.0 / D))
            VB(e, e.tensor_scalar_mul(COL["ex2"], COL["sumsq"], 1.0 / D))
            VB(e, e.tensor_mul(COL["tmp1"], COL["m"], COL["m"]))
            VB(e, e.tensor_sub(COL["var"], COL["ex2"], COL["tmp1"]))
            return VB(e, e.tensor_scalar_add(COL["vpe"], COL["var"], EPS))
        vv = S.add("vector", _statmath, [("cT", st_ids[-1])], inc="cV")

        sv = S.add("scalar",
                   lambda e: e.activation(COL["s0"], COL["vpe"], AF.Sqrt),
                   [("cV", vv)], inc="cSc")

        def _rstd(e):
            VBR(e, e.reciprocal(COL["tmp2"], COL["s0"]))
            VB(e, e.tensor_mul(COL["tmp2"], COL["vpe"], COL["tmp2"]))
            VB(e, e.tensor_add(COL["tmp1"], COL["s0"], COL["tmp2"]))
            VB(e, e.tensor_scalar_mul(COL["tmp1"], COL["tmp1"], 0.5))
            VBR(e, e.reciprocal(COL["rstd"], COL["tmp1"]))
            VBR(e, e.reciprocal(COL["tmp2"], COL["vpe"]))
            VB(e, e.tensor_mul(COL["tmp2"], COL["var"], COL["tmp2"]))
            return VB(e, e.tensor_scalar_add(COL["tmp2"], COL["tmp2"], EPS))
        vv = S.add("vector", _rstd, [("cSc", sv)], inc="cV")

        sv = S.add("scalar",
                   lambda e: e.activation(COL["s0"], COL["tmp2"], AF.Sqrt),
                   [("cV", vv)], inc="cSc")

        def _rms(e):
            VBR(e, e.reciprocal(COL["tmp1"], COL["s0"]))
            VB(e, e.tensor_mul(COL["tmp1"], COL["tmp2"], COL["tmp1"]))
            VB(e, e.tensor_add(COL["tmp1"], COL["s0"], COL["tmp1"]))
            VB(e, e.tensor_scalar_mul(COL["tmp1"], COL["tmp1"], 0.5))
            VBR(e, e.reciprocal(COL["rms"], COL["tmp1"]))
            return VB(e, e.tensor_mul(COL["mr"], COL["m"], COL["rstd"]))
        rms_v = S.add("vector", _rms, [("cSc", sv)], inc="cV")

        S.add("scalar", lambda e: e.preload_act_table(AF.Silu),
              [("cV", rms_v)])

        br_v = bcast(COL["rstd"], Bb1)
        bmr_v = bcast(COL["mr"], Bb2)

        m1ctx = ExitStack()
        m1acc = m1ctx.enter_context(
            nc.sbuf_tensor("m1acc", [128, NT], F32, side="right"))
        m1nacc = m1ctx.enter_context(
            nc.sbuf_tensor("m1nacc", [128, NT], F32, side="right"))

        def _xn(e):
            e.memset(m1acc[:], -1e30)
            e.memset(m1nacc[:], 1e30)
            for dc in range(NDC):
                sl = slice(dc * NT, (dc + 1) * NT)
                e.tensor_mul(scrA[:], s16[:, sl], Bb1[:])
                e.tensor_sub(scrA[:], scrA[:], Bb2[:])
                e.tensor_copy(xh[:, sl], scrA[:])
                e.tensor_sub(xl[:, sl], scrA[:], xh[:, sl])
                e.tensor_scalar(scrB[:], scrA[:], wn_s[:, dc:dc + 1], None,
                                ALU.mult)
                e.tensor_tensor(m1acc[:], m1acc[:], scrB[:], ALU.max)
                e.tensor_tensor(m1nacc[:], m1nacc[:], scrB[:], ALU.min)
                e.tensor_add(s16[:, sl], s16[:, sl], scrA[:])
            return None
        xn_v = S.add("vector", _xn, [("cV", bmr_v)], inc="cV")

        m1_ids = []
        for tc in range(8):
            ps = PS4[2 + tc % 2]
            w = [("cV", xn_v)]
            if tc >= 2:
                w.append(("cV", m1_ids[tc - 2][1]))

            def _m1t(e, tc=tc, ps=ps):
                e.transpose(ps[:, 0:128],
                            m1acc[:, tc * 128:(tc + 1) * 128], ident[:])
                return e.transpose(ps[:, 128:256],
                                   m1nacc[:, tc * 128:(tc + 1) * 128],
                                   ident[:])
            tid = S.add("tensor", _m1t, w, inc="cT")

            def _m1r(e, tc=tc, ps=ps):
                VB(e, e.reduce_max(COL["m1"][:, tc:tc + 1], ps[:, 0:128],
                                   axis=AX.X))
                return VB(e, e.tensor_reduce(COL["tmp1"][:, tc:tc + 1],
                                             ps[:, 128:256], axis=AX.X,
                                             op=ALU.min))
            vid = S.add("vector", _m1r, [("cT", tid)], inc="cV")
            m1_ids.append((tid, vid))

        def _sa(e):
            VB(e, e.tensor_scalar_mul(COL["tmp1"], COL["tmp1"], -1.0))
            VB(e, e.tensor_tensor(COL["m1"], COL["m1"], COL["tmp1"], ALU.max))
            VB(e, e.tensor_mul(COL["mxa"], COL["rms"], COL["m1"]))
            VB(e, e.tensor_scalar_max(COL["mxa"], COL["mxa"], EPS))
            VB(e, e.tensor_scalar_mul(COL["desca"], COL["mxa"], 1.0 / QB))
            VBR(e, e.reciprocal(COL["sca"], COL["mxa"]))
            VB(e, e.tensor_scalar_mul(COL["sca"], COL["sca"], QB))
            return VB(e, e.tensor_mul(COL["rs"], COL["rms"], COL["sca"]))
        sa_v = S.add("vector", _sa, [("cV", m1_ids[-1][1])], inc="cV")

        brs_v = bcast(COL["rs"], Bb1)

        aqctx = ExitStack()
        aqi8 = [aqctx.enter_context(
            nc.sbuf_tensor(f"aqi8_{i}", [128, NT], I8, side="right"))
            for i in range(2)]

        aq_v = None
        aq_dma = None
        aq_dmas = {}
        for dc in range(NDC):
            def _aqc(e, dc=dc):
                sl = slice(dc * NT, (dc + 1) * NT)
                e.tensor_add(scrA[:], xh[:, sl], xl[:, sl])
                e.tensor_scalar(scrA[:], scrA[:], wn_s[:, dc:dc + 1], None,
                                ALU.mult)
                e.tensor_mul(scrB[:], scrA[:], Bb1[:])
                return e.tensor_scalar(aqi8[dc % 2][:], scrB[:], MAGIC,
                                       -MAGIC, ALU.add, ALU.add)
            w = [("cV", brs_v)]
            if dc >= 2:
                w.append((f"cAQ{dc % 2}", aq_dmas[dc - 2]))
            aq_v = S.add("vector", _aqc, w, inc="cV")
            aq_dmas[dc] = S.dma("scalar", aq_sp[:, dc * NT:(dc + 1) * NT],
                                aqi8[dc % 2][:], waits=[("cV", aq_v)],
                                ch=f"cAQ{dc % 2}")
        aq_dma = aq_dmas[NDC - 1]

        _sync_sems()
        S.emit_partial(block, sems)
        aqctx.close()
        m1ctx.close()
        shctx.close()
        sqctx.close()
        AW = [("cAQ0", aq_dmas[NDC - 2]), ("cAQ1", aq_dmas[NDC - 1]),
              ("cV", aq_v), ("cT", m1_ids[-1][0]), ("cSc", abs_ids[-1])]

        bctx = ExitStack()
        BE = bctx.enter_context
        kfm = BE(nc.sbuf_tensor("kfm", [128, 512], F32, side="right"))
        kTh = [BE(nc.sbuf_tensor(f"kTh{i}", [128, 512], BF16, side="right"))
               for i in range(2)]
        kTl = [BE(nc.sbuf_tensor(f"kTl{i}", [128, 512], BF16, side="right"))
               for i in range(2)]
        vTh = [BE(nc.sbuf_tensor(f"vTh{i}", [128, 512], BF16, side="right"))
               for i in range(2)]
        vTl = [BE(nc.sbuf_tensor(f"vTl{i}", [128, 512], BF16, side="right"))
               for i in range(2)]
        red1 = BE(nc.sbuf_tensor("red1", [128, 2], F32, side="right"))
        ks2 = BE(nc.sbuf_tensor("ks2", [128, 32], F32, side="right"))
        t2raw = [BE(nc.sbuf_tensor(f"t2raw{i}", [128, 1024], F32, side="right"))
                 for i in range(2)]
        t2f = BE(nc.sbuf_tensor("t2f", [128, 1024], F32, side="right"))
        t2i = BE(nc.sbuf_tensor("t2i", [128, 1024], I8, side="right"))
        t2b = [BE(nc.sbuf_tensor(f"t2b{i}", [128, 1024], BF16, side="right"))
               for i in range(2)]

        kvps_v = S.add("vector",
                       lambda e: (e.memset(PBIG[:], 0.0),
                                  e.memset(ks_sb[:], 0.0))[-1],
                       AW, inc="cV")

        t2_state = {"i": 0, "ids": {}}

        def t2_step(n=1):
            for _ in range(n):
                i = t2_state["i"]
                if i >= 128:
                    return
                t2_state["i"] += 1
                hc, half = i // 2, i % 2
                dw = list(AW) if i < 2 else []
                if i >= 2:
                    dw.append(("cV", t2_state["ids"][i - 2][0]))
                did = S.dma(
                    "scalar", t2raw[i % 2][:],
                    woutT.rearrange("(j p) n -> p j n", p=128)[
                        :, hc, half * 1024:(half + 1) * 1024],
                    waits=dw, ch=f"cT2R{i % 2}")

                def _tern(e, i=i):
                    e.tensor_scalar(t2f[:], t2raw[i % 2][:], sw2[:, 1:2],
                                    1.0, ALU.mult, ALU.min)
                    e.tensor_scalar(t2f[:], t2f[:], -1.0, MAGIC,
                                    ALU.max, ALU.add)
                    return e.tensor_scalar(t2i[:], t2f[:], MAGIC, None,
                                           ALU.subtract)
                tv_ = S.add("vector", _tern,
                            [(f"cT2R{i % 2}", did), ("cV", sw_done)],
                            inc="cV")
                cpw = [("cV", tv_)]
                if i >= 2:
                    cpw.append((f"cT2S{i % 2}", t2_state["ids"][i - 2][1]))
                cp = S.add("vector",
                           (lambda e, i=i: e.tensor_copy(t2b[i % 2][:],
                                                         t2i[:])),
                           cpw, inc="cV")
                sod = S.dma("scalar",
                            t2_sp[hc][:, half * 1024:(half + 1) * 1024],
                            t2b[i % 2][:], waits=[("cV", cp)],
                            ch=f"cT2S{i % 2}")
                t2_state["ids"][i] = (cp, sod)

        split_ids = {}
        mm_done = {}
        fm_ids = {}
        tm_ids = {}
        kvmm_ids = {}
        kv_mm_last = None
        for i in range(32):
            is_k = (i % 2 == 0)
            ofb = i // 2
            wb = i % 2
            w_src = wk_d if is_k else wv_d
            dw = [("cV", split_ids[i - 2])] if i >= 2 else list(AW)
            did = S.dma("sync",
                        wraw[wb][:].rearrange("p (c n) -> p c n", c=NDC),
                        w_src.rearrange("(c p) n -> p c n", p=128)[
                            :, :, ofb * 128:(ofb + 1) * 128],
                        waits=dw, ch=f"cWL{wb}")

            def _wsplit(e, wb=wb):
                e.tensor_copy(wsph[wb][:], wraw[wb][:])
                return e.tensor_sub(wspl[wb][:], wraw[wb][:], wsph[wb][:])
            sw_ = [(f"cWL{wb}", did)]
            if i >= 2:
                sw_.append(("cT", mm_done[i - 2]))
            split_ids[i] = S.add("vector", _wsplit, sw_, inc="cV")

            for tq in range(2):
                pp = PA if tq == 0 else PB

                def _proj(e, wb=wb, tq=tq, pp=pp):
                    n = 0
                    bi = None
                    for dc in range(NDC):
                        lh = wsph[wb][:, dc * 128:(dc + 1) * 128]
                        ll = wspl[wb][:, dc * 128:(dc + 1) * 128]
                        for (lt, rt_) in ((lh, xh), (lh, xl), (ll, xh)):
                            n += 1
                            bi = e.matmul(
                                pp[:], lt,
                                rt_[:, dc * NT + tq * 512:
                                    dc * NT + (tq + 1) * 512],
                                start=(n == 1), stop=(n == 3 * NDC))
                    return bi
                pw = [("cV", split_ids[i]), ("cV", kvps_v), ("cV", aq_v)]
                if (i - 1, tq) in fm_ids:
                    pw.append(("cV", fm_ids[(i - 1, tq)]))
                proj_t = S.add("tensor", _proj, pw, inc="cT")
                mm_done[i] = proj_t

                def _fm(e, pp=pp, is_k=is_k, ofb=ofb, tq=tq):
                    if is_k:
                        e.tensor_scalar(kfm[:], pp[:], 0.0, None, ALU.max)
                        return VB(e, e.reduce_sum(
                            ks2[:, 16 * tq + ofb:16 * tq + ofb + 1], kfm[:],
                            axis=AX.X))
                    return e.tensor_copy(kfm[:], pp[:])
                fw = [("cT", proj_t)]
                prev = (i, tq - 1) if tq == 1 else (i - 1, 1)
                if prev in tm_ids:
                    fw.append(("cT", tm_ids[prev][0]))
                fm_ids[(i, tq)] = S.add("vector", _fm, fw, inc="cV")

                pt = PC if tq == 0 else PD

                def _tp(e, pt=pt):
                    bi = None
                    for b4 in range(4):
                        bi = e.transpose(pt[:, b4 * 128:(b4 + 1) * 128],
                                         kfm[:, b4 * 128:(b4 + 1) * 128],
                                         ident[:])
                    return bi
                tw = [("cV", fm_ids[(i, tq)])]
                prevtm = tm_ids.get(prev)
                if prevtm is not None:
                    tw.append(("cV", prevtm[1]))
                tp_l = S.add("tensor", _tp, tw, inc="cT")

                dsth = (kTh if is_k else vTh)[tq]
                dstl = (kTl if is_k else vTl)[tq]

                def _tm(e, dsth=dsth, dstl=dstl, pt=pt):
                    e.tensor_copy(dsth[:], pt[:])
                    return e.tensor_tensor(dstl[:], pt[:], dsth[:],
                                           ALU.subtract)
                tmw = [("cT", tp_l)]
                prev_kv = (i - 1, tq) if is_k else (i - 2, tq)
                if prev_kv in kvmm_ids:
                    tmw.append(("cT", kvmm_ids[prev_kv]))
                tm_v = S.add("vector", _tm, tmw, inc="cV")
                tm_ids[(i, tq)] = (tp_l, tm_v)

                if not is_k:
                    def _kvmm(e, ofb=ofb, tq=tq):
                        bi = None
                        for tch in range(4):
                            for hh in range(2):
                                rb = 64 * hh
                                cb = ofb * 128 + 64 * hh
                                ksl = slice(tch * 128 + hh * 64,
                                            tch * 128 + hh * 64 + 64)
                                st0 = (ofb % 4 == 0 and tq == 0
                                       and tch == 0)
                                for ip, (lt, rt_) in enumerate(
                                        ((kTh[tq], vTh[tq]),
                                         (kTh[tq], vTl[tq]),
                                         (kTl[tq], vTh[tq]))):
                                    bi = e.matmul(
                                        PBIG[rb:rb + 64, cb:cb + 64],
                                        lt[:, ksl], rt_[:, ksl],
                                        start=(st0 and ip == 0),
                                        stop=False,
                                        skip_group_check=True)
                        return bi
                    kvmm_ids[(i, tq)] = S.add(
                        "tensor", _kvmm,
                        [("cV", tm_v), ("cV", tm_ids[(i - 1, tq)][1])],
                        inc="cT")
                    kv_mm_last = kvmm_ids[(i, tq)]
            t2_step(4)

        t2_last_spill = t2_state["ids"][127][1]
        _sync_sems()
        S.emit_partial(block, sems)
        bctx.close()

        fctx2 = ExitStack()
        F2 = fctx2.enter_context
        kvh = F2(nc.sbuf_tensor("kvh", [128, 2048], BF16, side="right"))
        kvl = F2(nc.sbuf_tensor("kvl", [128, 2048], BF16, side="right"))
        kshi = F2(nc.sbuf_tensor("kshi", [128, 16], BF16, side="right"))
        kslo = F2(nc.sbuf_tensor("kslo", [128, 16], BF16, side="right"))
        ksdh = F2(nc.sbuf_tensor("ksdh", [128, NPAIR * 32], BF16, side="right"))
        ksdl = F2(nc.sbuf_tensor("ksdl", [128, NPAIR * 32], BF16, side="right"))
        kvfctx = ExitStack()
        kv_f = kvfctx.enter_context(
            nc.sbuf_tensor("kv_f", [128, 2080], F32, side="right"))

        def _ccevac(e):
            e.tensor_copy(kv_f[:, 0:2048], PBIG[:])
            VB(e, e.tensor_tensor(ks_sb[:], ks2[:, 0:16], ks2[:, 16:32],
                                  ALU.add))
            e.tensor_copy(kv_f[:, 2048:2064], ks_sb[:])
            return e.memset(kv_f[:, 2064:2080], 0.0)
        vv = S.add("vector", _ccevac,
                   [("cT", kv_mm_last), ("cT2S0", t2_state["ids"][126][1]),
                    ("cT2S1", t2_state["ids"][127][1])], inc="cV")
        gd = S.dma("gpsimd", ccB_in[:], kv_f[:], waits=[("cV", vv)])
        if DBG:
            S.dma("scalar", dbg_ksin[:], kv_f[:, 2048:2064],
                  waits=[("cV", vv)], ch="cOUT")
        cc = S.add("gpsimd",
                   lambda e: e.collective_compute(
                       "AllReduce", ALU.add,
                       replica_groups=[[0, 1], [2, 3], [4, 5], [6, 7]],
                       ins=[ccB_in[:].opt()], outs=[ccB_out[:].opt()]),
                   [("cG", gd)], inc="cCC")
        gd = S.dma("gpsimd", kv_f[:], ccB_out[:], waits=[("cCC", cc)])

        def _kvsplit(e):
            e.tensor_copy(kvh[:], kv_f[:, 0:2048])
            VB(e, e.tensor_sub(kvl[:], kv_f[:, 0:2048], kvh[:]))
            VB(e, e.tensor_copy(ks_sb[:], kv_f[:, 2048:2064]))
            VB(e, e.tensor_copy(kshi[:], ks_sb[:]))
            VB(e, e.tensor_sub(kslo[:], ks_sb[:], kshi[:]))
            VB(e, e.memset(ksdh[:], 0.0))
            VB(e, e.memset(ksdl[:], 0.0))
            bi = None
            for dc in range(NDC):
                c0 = dc * 32 + 2 * dc
                e.tensor_copy(ksdh[0:64, c0:c0 + 1], kshi[0:64, dc:dc + 1])
                e.tensor_copy(ksdh[64:128, c0 + 1:c0 + 2],
                              kshi[64:128, dc:dc + 1])
                e.tensor_copy(ksdl[0:64, c0:c0 + 1], kslo[0:64, dc:dc + 1])
                bi = e.tensor_copy(ksdl[64:128, c0 + 1:c0 + 2],
                                   kslo[64:128, dc:dc + 1])
            VB(e, bi)
            return None
        kvsp_v = S.add("vector", _kvsplit, [("cG", gd)], inc="cV")
        if DBG:
            S.dma("scalar", dbg_ksd[:, 0:NPAIR * 32], ksdh[:],
                  waits=[("cV", kvsp_v)], ch="cOUT")
            S.dma("scalar", dbg_ksd[:, NPAIR * 32:], ksdl[:],
                  waits=[("cV", kvsp_v)], ch="cOUT")
        if DBG:
            S.dma("scalar", dbg_kv[:], kv_f[:], waits=[("cG", gd)],
                  ch="cOUT")

        _sync_sems()
        S.emit_partial(block, sems)
        kvfctx.close()

        fctx = ExitStack()
        FE = fctx.enter_context
        qch = FE(nc.sbuf_tensor("qch", [128, FQ], BF16, side="right"))
        qcl = FE(nc.sbuf_tensor("qcl", [128, FQ], BF16, side="right"))
        rnf = FE(nc.sbuf_tensor("rnf", [32, FQ], F32, side="right"))
        rnh = FE(nc.sbuf_tensor("rnh", [32, FQ], BF16, side="right"))
        rnl = FE(nc.sbuf_tensor("rnl", [32, FQ], BF16, side="right"))
        oh = FE(nc.sbuf_tensor("oh", [128, NDC * FQ], BF16, side="right"))
        ol = FE(nc.sbuf_tensor("ol", [128, NDC * FQ], BF16, side="right"))
        of32 = FE(nc.sbuf_tensor("of32", [128, FQ], F32, side="right"))
        brn_sb = FE(nc.sbuf_tensor("brn_sb", [128, FQ], F32, side="right"))

        rn_init = S.add("vector",
                        lambda e: (e.memset(rnf[:], 1.0),
                                   e.memset(rnh[:], 0.0),
                                   e.memset(rnl[:], 0.0))[-1],
                        [("cV", kvsp_v)], inc="cV")

        NTQ = NT // FQ
        fq_split = {}
        qsplit_ids = {}
        osplit_ids = {}
        qkv_ids = {}
        wo_split = {}
        wo_mm = {}
        wo_evac = {}
        for tq in range(NTQ):
            for j in range(NPAIR):
                wb = j % 2
                dw = []
                if (tq, j - 2) in fq_split:
                    dw.append(("cV", fq_split[(tq, j - 2)]))
                elif tq > 0:
                    dw.append(("cV", wo_split[(tq - 1, NPAIR - 2 + j)]))
                else:
                    dw.append(("cV", split_ids[30 + j]))
                did = S.dma("sync",
                            wraw[wb][:].rearrange("p (c n) -> p c n", c=NDC),
                            wq_d.rearrange("(c p) n -> p c n", p=128)[
                                :, :, j * 128:(j + 1) * 128],
                            waits=dw, ch=f"cWL{wb}")

                def _wsplit(e, wb=wb):
                    e.tensor_copy(wsph[wb][:], wraw[wb][:])
                    return e.tensor_sub(wspl[wb][:], wraw[wb][:],
                                        wsph[wb][:])
                sw_ = [(f"cWL{wb}", did)]
                if (tq, j - 2) in qkv_ids:
                    sw_.append(("cT", qkv_ids[(tq, j - 2)]))
                elif tq > 0:
                    sw_.append(("cT", wo_mm[(tq - 1, NPAIR - 2 + j)]))
                else:
                    sw_.append(("cT", mm_done[30 + j]))
                fq_split[(tq, j)] = S.add("vector", _wsplit, sw_, inc="cV")

                psl = PBIG[:, (j % 4) * 512:(j % 4) * 512 + 512]

                def _qproj(e, wb=wb, tq=tq, psl=psl):
                    n = 0
                    bi = None
                    for dc in range(NDC):
                        lh = wsph[wb][:, dc * 128:(dc + 1) * 128]
                        ll = wspl[wb][:, dc * 128:(dc + 1) * 128]
                        for (lt, rt_) in ((lh, xh), (lh, xl), (ll, xh)):
                            n += 1
                            bi = e.matmul(
                                psl[:, 0:FQ], lt,
                                rt_[:, dc * NT + tq * FQ:
                                    dc * NT + (tq + 1) * FQ],
                                start=(n == 1), stop=(n == 3 * NDC))
                    return bi
                qw = [("cV", fq_split[(tq, j)]), ("cV", rn_init)]
                if (tq, j - 4) in qsplit_ids:
                    qw.append(("cV", qsplit_ids[(tq, j - 4)]))
                elif tq > 0:
                    qw.append(("cV", wo_evac[(tq - 1, NPAIR - 1)]))
                qp_t = S.add("tensor", _qproj, qw, inc="cT")

                def _qsplit(e, psl=psl):
                    e.tensor_scalar(qch[:], psl[:, 0:FQ], 0.0, None,
                                    ALU.max)
                    return e.scalar_tensor_tensor(qcl[:], psl[:, 0:FQ], 0.0,
                                                  qch[:], ALU.max,
                                                  ALU.subtract)
                qsw = [("cT", qp_t)]
                if (tq, j - 1) in qkv_ids:
                    qsw.append(("cT", qkv_ids[(tq, j - 1)]))
                qsplit_ids[(tq, j)] = S.add("vector", _qsplit, qsw,
                                            inc="cV")

                PQ = PA if j % 2 == 0 else PB
                PN = PC if j % 2 == 0 else PD

                def _qkvmm(e, j=j, PQ=PQ, PN=PN):
                    kb = slice(j * 128, (j + 1) * 128)
                    e.matmul(PQ[:, 0:FQ], kvh[:, kb], qch[:], start=True,
                             stop=False)
                    e.matmul(PQ[:, 0:FQ], kvh[:, kb], qcl[:], start=False,
                             stop=False)
                    e.matmul(PQ[:, 0:FQ], kvl[:, kb], qch[:], start=False,
                             stop=True)
                    kd = slice(j * 32, (j + 1) * 32)
                    e.matmul(PN[0:32, 0:FQ], ksdh[:, kd], qch[:],
                             start=True, stop=False)
                    e.matmul(PN[0:32, 0:FQ], ksdh[:, kd], qcl[:],
                             start=False, stop=False)
                    return e.matmul(PN[0:32, 0:FQ], ksdl[:, kd], qch[:],
                                    start=False, stop=True)
                kw = [("cV", qsplit_ids[(tq, j)])]
                if (tq, j - 2) in osplit_ids:
                    kw.append(("cV", osplit_ids[(tq, j - 2)]))
                qkv_ids[(tq, j)] = S.add("tensor", _qkvmm, kw, inc="cT")

                def _rn(e, j=j, PN=PN):
                    VB(e, e.tensor_scalar(rnf[:], PN[0:32, 0:FQ], 1e-6, None,
                                          ALU.max))
                    VBR(e, e.reciprocal(rnf[:], rnf[:]))
                    VB(e, e.tensor_copy(rnh[:], rnf[:]))
                    return VB(e, e.tensor_sub(rnl[:], rnf[:], rnh[:]))
                rn_v = S.add("vector", _rn, [("cT", qkv_ids[(tq, j)])],
                             inc="cV")
                if DBG and tq == 0 and j <= 1:
                    S.dma("scalar",
                          dbg_rn[:, j * FQ:(j + 1) * FQ],
                          rnf[:], waits=[("cV", rn_v)], ch="cOUT")

                def _exp(e, j=j, PN=PN):
                    eb = slice(j * 128, (j + 1) * 128)
                    e.matmul(PN[:, 0:FQ], ediag[:, eb], rnh[:], start=True,
                             stop=False)
                    return e.matmul(PN[:, 0:FQ], ediag[:, eb], rnl[:],
                                    start=False, stop=True)
                exp_t = S.add("tensor", _exp, [("cV", rn_v)], inc="cT")

                def _osplit(e, j=j, PQ=PQ, PN=PN):
                    osl = slice(j * FQ, (j + 1) * FQ)
                    e.tensor_copy(brn_sb[:], PN[:, 0:FQ])
                    e.tensor_mul(of32[:], PQ[:, 0:FQ], brn_sb[:])
                    e.tensor_copy(oh[:, osl], of32[:])
                    return e.tensor_sub(ol[:, osl], of32[:], oh[:, osl])
                osplit_ids[(tq, j)] = S.add("vector", _osplit,
                                            [("cT", exp_t)], inc="cV")
                if DBG and tq == 0 and j == 0:
                    S.dma("scalar", dbg_o[:, 0:FQ], of32[:],
                          waits=[("cV", osplit_ids[(tq, j)])], ch="cOUT")
                    S.dma("scalar", dbg_o[0:128, FQ:2 * FQ].bitcast(F32)
                          if False else dbg_o[:, FQ:2 * FQ],
                          brn_sb[:],
                          waits=[("cV", osplit_ids[(tq, j)])], ch="cOUT")
                t2_step(1)

            for oc in range(NPAIR):
                wb = oc % 2
                dw = [("cV", wo_split[(tq, oc - 2)])] if oc >= 2 else \
                    [("cV", fq_split[(tq, NPAIR - 2 + oc)])]
                did = S.dma("sync",
                            wraw[wb][:].rearrange("p (c n) -> p c n", c=NDC),
                            wo_d.rearrange("(c p) n -> p c n", p=128)[
                                :, :, oc * 128:(oc + 1) * 128],
                            waits=dw, ch=f"cWL{wb}")

                def _wsplit(e, wb=wb):
                    e.tensor_copy(wsph[wb][:], wraw[wb][:])
                    return e.tensor_sub(wspl[wb][:], wraw[wb][:],
                                        wsph[wb][:])
                sw_ = [(f"cWL{wb}", did), ("cV", osplit_ids[(tq, NPAIR - 1)])]
                if oc >= 2:
                    sw_.append(("cT", wo_mm[(tq, oc - 2)]))
                else:
                    sw_.append(("cT", qkv_ids[(tq, NPAIR - 1)]))
                wo_split[(tq, oc)] = S.add("vector", _wsplit, sw_, inc="cV")

                psl = PBIG[:, (oc % 4) * 512:(oc % 4) * 512 + 512]

                def _womm(e, wb=wb, psl=psl):
                    n = 0
                    bi = None
                    for dcI in range(NDC):
                        lh = wsph[wb][:, dcI * 128:(dcI + 1) * 128]
                        ll = wspl[wb][:, dcI * 128:(dcI + 1) * 128]
                        osl = slice(dcI * FQ, (dcI + 1) * FQ)
                        for (lt, rt_) in ((lh, oh), (lh, ol), (ll, oh)):
                            n += 1
                            bi = e.matmul(psl[:, 0:FQ], lt, rt_[:, osl],
                                          start=(n == 1), stop=(n == 3 * NDC))
                    return bi
                ww = [("cV", wo_split[(tq, oc)])]
                if (tq, oc - 4) in wo_evac:
                    ww.append(("cV", wo_evac[(tq, oc - 4)]))
                wo_mm[(tq, oc)] = S.add("tensor", _womm, ww, inc="cT")

                def _woev(e, oc=oc, tq=tq, psl=psl):
                    sl = slice(oc * NT + tq * FQ, oc * NT + (tq + 1) * FQ)
                    return e.scalar_tensor_tensor(
                        s16[:, sl], psl[:, 0:FQ], bo_s[:, oc:oc + 1],
                        s16[:, sl], ALU.add, ALU.add)
                wo_evac[(tq, oc)] = S.add("vector", _woev,
                                          [("cT", wo_mm[(tq, oc)])],
                                          inc="cV")
                t2_step(1)
        t2_step(128)
        f_last_t = wo_mm[(NTQ - 1, NPAIR - 1)]
        f_last_v = wo_evac[(NTQ - 1, NPAIR - 1)]

        _sync_sems()
        S.emit_partial(block, sems)
        fctx.close()
        fctx2.close()
        xhctx.close()

        dctx = ExitStack()
        DE = dctx.enter_context
        abf = DE(nc.sbuf_tensor("abf", [128, NDC * NT], BF16, side="right"))
        aqc = DE(nc.sbuf_tensor("aqc", [128, NT], I8, side="right"))
        t1raw = wraw
        t1f = DE(nc.sbuf_tensor("t1f", [128, 2048], F32, side="right"))
        t1i = DE(nc.sbuf_tensor("t1i", [128, 2048], I8, side="right"))
        t1b = [DE(nc.sbuf_tensor(f"t1b{i}", [128, 2048], BF16, side="right"))
               for i in range(7)]
        sgt = DE(nc.sbuf_tensor("sgt", [128, 512], F32, side="right"))
        gvv = DE(nc.sbuf_tensor("gvv", [128, 512], F32, side="right"))
        gvg = DE(nc.sbuf_tensor("gvg", [128, 512], F32, side="right"))
        ut = [DE(nc.sbuf_tensor(f"ut{i}", [128, 512], F32, side="right")) for i in range(4)]
        mxacc = DE(nc.sbuf_tensor("mxacc", [128, NT], F32, side="right"))
        mnacc = DE(nc.sbuf_tensor("mnacc", [128, NT], F32, side="right"))

        DW = [("cT", f_last_t), ("cV", f_last_v)]
        ab_ids = []
        for dc in range(NDC):
            dw = [("cV", ab_ids[dc - 1])] if dc >= 1 else \
                [("cAQ0", aq_dmas[NDC - 2]), ("cAQ1", aq_dmas[NDC - 1])] + DW
            did = S.dma("scalar", aqc[:],
                        aq_sp[:, dc * NT:(dc + 1) * NT], waits=dw,
                        ch="cAQC")
            ab_ids.append(S.add(
                "vector",
                (lambda e, dc=dc: e.tensor_copy(
                    abf[:, dc * NT:(dc + 1) * NT], aqc[:])),
                [("cAQC", did)], inc="cV"))
        abf_v = ab_ids[-1]

        cda_v = S.add("vector",
                      lambda e: VB(e, e.tensor_scalar(
                          COL["cda"], COL["desca"], sw2[:, 2:3], None,
                          ALU.mult)),
                      [("cV", sa_v), ("cV", sw_done), ("cV", f_last_v)],
                      inc="cV")
        bcast(COL["sum"], Bb2)
        bc_v = bcast(COL["cda"], Bb1)
        mx_init = S.add("vector",
                        lambda e: (e.memset(mxacc[:], -1e30),
                                   e.memset(mnacc[:], 1e30))[-1],
                        DW, inc="cV")

        t1_ids = {}
        gvmm_ids = {}
        uev_ids = {}
        usp_ids = {}
        for ic in range(128):
            hc = ic // 2
            is_gate = ic % 2 == 1
            col = (64 + hc) * 128 if is_gate else hc * 128
            dw = [("cV", t1_ids[ic - 2][0])] if ic >= 2 else list(DW)
            did = S.dma("sync",
                        t1raw[ic % 2][:].rearrange("p (c n) -> p c n",
                                                   c=NDC),
                        wgvT.rearrange("(c p) n -> p c n", p=128)[
                            :, :, col:col + 128],
                        waits=dw, ch=f"cWL{ic % 2}")

            def _tern(e, ic=ic):
                e.tensor_scalar(t1f[:], t1raw[ic % 2][:], sw2[:, 0:1], 1.0,
                                ALU.mult, ALU.min)
                e.tensor_scalar(t1f[:], t1f[:], -1.0, MAGIC,
                                ALU.max, ALU.add)
                return e.tensor_scalar(t1i[:], t1f[:], MAGIC, None,
                                       ALU.subtract)
            tv_ = S.add("vector", _tern,
                        [(f"cWL{ic % 2}", did), ("cV", sw_done)], inc="cV")
            slot = ic % 7

            def _cp(e, slot=slot):
                return e.tensor_copy(t1b[slot][:], t1i[:])
            cpw = [("cV", tv_)]
            if ic >= 7:
                cpw.append(("cT", gvmm_ids[ic - 7]))
            cp = S.add("vector", _cp, cpw, inc="cV")
            t1_ids[ic] = (tv_, cp)

            if is_gate:
                vslot = (ic - 1) % 7
                gslot = ic % 7
                for th in range(2):
                    pv = PS4[2 * (th % 2)]
                    pg = PS4[2 * (th % 2) + 1]

                    def _gvmm(e, vslot=vslot, gslot=gslot, th=th, hc=hc,
                              pv=pv, pg=pg):
                        bi = None
                        for dc in range(NDC):
                            e.matmul(pv[:],
                                     t1b[vslot][:, dc * 128:(dc + 1) * 128],
                                     abf[:, dc * NT + th * 512:
                                         dc * NT + (th + 1) * 512],
                                     start=(dc == 0), stop=(dc == NDC - 1))
                        for dc in range(NDC):
                            bi = e.matmul(
                                pg[:],
                                t1b[gslot][:, dc * 128:(dc + 1) * 128],
                                abf[:, dc * NT + th * 512:
                                    dc * NT + (th + 1) * 512],
                                start=(dc == 0), stop=(dc == NDC - 1))
                        return bi
                    gw = [("cV", cp), ("cV", abf_v)]
                    key = (hc - 1, th)
                    if key in uev_ids:
                        gw.append(("cV", uev_ids[key]))
                    gmm = S.add("tensor", _gvmm, gw, inc="cT")
                    if th == 1:
                        gvmm_ids[ic] = gmm
                        gvmm_ids[ic - 1] = gmm

                    def _uev(e, pv=pv, pg=pg, th=th, hc=hc):
                        us = ut[(2 * hc + th) % 4]
                        e.tensor_mul(gvv[:], pv[:], Bb1[:,
                                                       th * 512:(th + 1) * 512])
                        return e.tensor_mul(gvg[:], pg[:],
                                            Bb1[:, th * 512:(th + 1) * 512])
                    ue1 = S.add("vector", _uev,
                                [("cT", gmm), ("cV", bc_v), ("cV", mx_init)],
                                inc="cV")
                    if DBG and hc == 0:
                        S.dma("scalar", dbg_gv[:, NT + th * 512:
                                               NT + (th + 1) * 512],
                              gvv[:], waits=[("cV", ue1)], ch="cOUT")
                        S.dma("scalar", dbg_gv[:, 2 * NT + th * 512:
                                               2 * NT + (th + 1) * 512],
                              gvg[:], waits=[("cV", ue1)], ch="cOUT")
                        if th == 0:
                            S.dma("scalar", dbg_gv[:, 0:NT], Bb1[:],
                                  waits=[("cV", bc_v)], ch="cOUT")
                    sg_s = S.add("scalar",
                                 lambda e: e.activation(sgt[:], gvg[:],
                                                        AF.Silu),
                                 [("cV", ue1)], inc="cSc")

                    def _umul(e, hc=hc, th=th):
                        us = ut[(2 * hc + th) % 4]
                        e.tensor_mul(us[:], gvv[:], sgt[:])
                        sl = slice(th * 512, (th + 1) * 512)
                        e.tensor_tensor(mxacc[:, sl], mxacc[:, sl], us[:],
                                        ALU.max)
                        return e.tensor_tensor(mnacc[:, sl], mnacc[:, sl],
                                               us[:], ALU.min)
                    uw = [("cSc", sg_s)]
                    key2 = (2 * hc + th) - 4
                    if key2 >= 0 and key2 in usp_ids:
                        uw.append((f"cUS{key2 % 4}", usp_ids[key2]))
                    ue2 = S.add("vector", _umul, uw, inc="cV")
                    uev_ids[(hc, th)] = ue2
                    usp_ids[2 * hc + th] = S.dma(
                        "scalar", u_sp[hc][:, th * 512:(th + 1) * 512],
                        ut[(2 * hc + th) % 4][:], waits=[("cV", ue2)],
                        ch=f"cUS{(2 * hc + th) % 4}")
        d_last_uev = uev_ids[(NHC - 1, 1)]

        mxt_ids = []
        for tc in range(8):
            ps = PS4[2 + tc % 2]
            w = [("cV", d_last_uev)]
            if tc >= 2:
                w.append(("cV", mxt_ids[tc - 2][1]))

            def _mxt(e, tc=tc, ps=ps):
                e.transpose(ps[:, 0:128],
                            mxacc[:, tc * 128:(tc + 1) * 128], ident[:])
                return e.transpose(ps[:, 128:256],
                                   mnacc[:, tc * 128:(tc + 1) * 128],
                                   ident[:])
            tid = S.add("tensor", _mxt, w, inc="cT")

            def _mxr(e, tc=tc, ps=ps):
                VB(e, e.reduce_max(COL["mxu"][:, tc:tc + 1], ps[:, 0:128],
                                   axis=AX.X))
                return VB(e, e.tensor_reduce(COL["tmp1"][:, tc:tc + 1],
                                             ps[:, 128:256], axis=AX.X,
                                             op=ALU.min))
            vid = S.add("vector", _mxr, [("cT", tid)], inc="cV")
            mxt_ids.append((tid, vid))

        def _su(e):
            VB(e, e.tensor_scalar_mul(COL["tmp1"], COL["tmp1"], -1.0))
            VB(e, e.tensor_tensor(COL["mxu"], COL["mxu"], COL["tmp1"],
                                  ALU.max))
            VB(e, e.tensor_scalar_max(COL["mxu"], COL["mxu"], EPS))
            VB(e, e.tensor_scalar_mul(COL["desu"], COL["mxu"], 1.0 / QB))
            VBR(e, e.reciprocal(COL["su"], COL["mxu"]))
            VB(e, e.tensor_scalar_mul(COL["su"], COL["su"], QB))
            return VB(e, e.tensor_scalar(COL["c2"], COL["desu"],
                                         sw2[:, 3:4], None, ALU.mult))
        su_v = S.add("vector", _su, [("cV", mxt_ids[-1][1])], inc="cV")
        bsu_v = bcast(COL["su"], Bb1)
        bc2_v = bcast(COL["c2"], Bb2)

        _sync_sems()
        S.emit_partial(block, sems)
        dctx.close()
        wctx.close()
        GW = ([("cT", mxt_ids[-1][0]), ("cT", gvmm_ids[127])] +
              [(f"cUS{s}", usp_ids[2 * NHC - 4 + s]) for s in range(4)])

        gctx = ExitStack()
        GE = gctx.enter_context
        T2 = [GE(nc.sbuf_tensor(f"T2_{i}", [128, 2048], BF16, side="right"))
              for i in range(12)]
        urb = [GE(nc.sbuf_tensor(f"urb{i}", [128, NT], F32, side="right"))
               for i in range(8)]
        aub = [GE(nc.sbuf_tensor(f"aub{i}", [128, NT], BF16, side="right"))
               for i in range(8)]
        q8c = GE(nc.sbuf_tensor("q8c", [128, NT], I8, side="right"))
        evt = GE(nc.sbuf_tensor("evt", [128, 512], F32, side="right"))

        NR = 8
        t2_dma = {}
        u_dma = {}
        quant_ids = {}
        round_mm_last = {}
        evac_ids = {}
        for r in range(NR):
            for k in range(8):
                hc = 8 * r + k
                dw = [("cV", bsu_v), ("cV", bc2_v)] + (GW if hc < 2 else [])
                if hc >= 12:
                    dw.append(("cT", round_mm_last[(hc - 12) // 8]))
                t2_dma[hc] = S.dma("scalar", T2[hc % 12][:],
                                   t2_sp[hc][:, :], waits=dw,
                                   ch=f"cTL{hc % 12}")
                uw = list(dw)
                if hc >= 8:
                    uw.append(("cV", quant_ids[hc - 8]))
                u_dma[hc] = S.dma("scalar", urb[hc % 8][:],
                                  u_sp[hc][:, :], waits=uw,
                                  ch=f"cUR{hc % 8}")

                def _quant(e, hc=hc):
                    e.tensor_mul(evt[:, 0:512], urb[hc % 8][:, 0:512],
                                 Bb1[:, 0:512])
                    e.tensor_scalar(q8c[:, 0:512], evt[:, 0:512], MAGIC,
                                    -MAGIC, ALU.add, ALU.add)
                    e.tensor_mul(evt[:, 0:512], urb[hc % 8][:, 512:1024],
                                 Bb1[:, 512:1024])
                    e.tensor_scalar(q8c[:, 512:1024], evt[:, 0:512], MAGIC,
                                    -MAGIC, ALU.add, ALU.add)
                    return e.tensor_copy(aub[hc % 8][:], q8c[:])
                qw = [(f"cUR{hc % 8}", u_dma[hc]), ("cV", bsu_v)]
                if hc >= 8:
                    qw.append(("cT", round_mm_last.get(r - 1)))
                quant_ids[hc] = S.add("vector", _quant, qw, inc="cV")

            last = None
            for b in range(8):
                groups = [(4 * b + g) for g in range(4)]
                for k in range(8):
                    hc = 8 * r + k

                    def _mm(e, groups=groups, hc=hc, k=k):
                        bi = None
                        for g in groups:
                            oc, th = g // 2, g % 2
                            bank = g % 8
                            ps = (PBIG[:, (bank % 4) * 512:
                                       (bank % 4) * 512 + 512]
                                  if bank < 4 else PS4[bank - 4][:])
                            bi = e.matmul(
                                ps, T2[hc % 12][:, oc * 128:(oc + 1) * 128],
                                aub[hc % 8][:, th * 512:(th + 1) * 512],
                                start=(k == 0), stop=(k == 7))
                        return bi
                    mw = [("cV", quant_ids[hc]),
                          (f"cTL{hc % 12}", t2_dma[hc])]
                    if b >= 2:
                        mw.append(("cV", evac_ids[(r, b - 2)]))
                    elif r >= 1:
                        mw.append(("cV", evac_ids[(r - 1, b + 6)]))
                    last = S.add("tensor", _mm, mw, inc="cT")
                round_mm_last[r] = last

                def _evac(e, groups=groups, r=r):
                    bi = None
                    for g in groups:
                        oc, th = g // 2, g % 2
                        bank = g % 8
                        ps = (PBIG[:, (bank % 4) * 512:
                                   (bank % 4) * 512 + 512]
                              if bank < 4 else PS4[bank - 4][:])
                        sl = slice(oc * NT + th * 512,
                                   oc * NT + (th + 1) * 512)
                        e.tensor_mul(evt[:], ps, Bb2[:,
                                                     th * 512:(th + 1) * 512])
                        bi = e.tensor_add(s16[:, sl], s16[:, sl], evt[:])
                    return bi
                evac_ids[(r, b)] = S.add("vector", _evac, [("cT", last)],
                                         inc="cV")

        if DBG:
            S.dma("scalar", dbg_st[:, 0:176], st[:],
                  waits=[("cV", evac_ids[(NR - 1, 7)])], ch="cOUT")
            S.dma("scalar", dbg_st[:, 176:180], sw2[:], ch="cOUT")
        out_dma = S.dma("scalar",
                        yT.rearrange("(c p) n -> p c n", p=128),
                        s16[:].rearrange("p (c n) -> p c n", c=NDC),
                        waits=[("cV", evac_ids[(NR - 1, 7)])], ch="cOUT")
        S.add("gpsimd", lambda e: None, [("cOUT", out_dma)])

        _sync_sems()
        S.emit_partial(block, sems)
        gctx.close()

    return nc


_NC_CACHE = {}


def kernel(**inputs):
    x = np.ascontiguousarray(np.asarray(inputs["L0_attn_hidden_states"],
                                        dtype=np.float32))
    wq = np.ascontiguousarray(np.asarray(inputs["L0_attn_Wq"], np.float32))
    wk = np.ascontiguousarray(np.asarray(inputs["L0_attn_Wk"], np.float32))
    wv = np.ascontiguousarray(np.asarray(inputs["L0_attn_Wv"], np.float32))
    wo = np.ascontiguousarray(np.asarray(inputs["L0_attn_Wo"], np.float32))
    bo = np.asarray(inputs["L0_attn_bo"], np.float32)
    wn = np.asarray(inputs["L0_ffn_w_norm"], np.float32)
    wgv = np.asarray(inputs["L0_ffn_w_gv"], np.float32)
    wout = np.asarray(inputs["L0_ffn_w_out"], np.float32)

    wgvT = np.ascontiguousarray(wgv.T)
    woutT = np.ascontiguousarray(wout.T)
    bo_t = np.ascontiguousarray(bo.reshape(NDC, 128).T)
    wn_t = np.ascontiguousarray(wn.reshape(NDC, 128).T)

    nc = build_program()

    in_maps = []
    for c in range(8):
        b, h = c // 2, c % 2
        xT = np.ascontiguousarray(x[b, h * NT:(h + 1) * NT, :].T)
        in_maps.append({
            "xT": xT, "wq": wq, "wk": wk, "wv": wv, "wo": wo,
            "bo_t": bo_t, "wn_t": wn_t, "wgvT": wgvT, "woutT": woutT,
            "gv_sh": np.ascontiguousarray(wgvT[:, c * 2048:(c + 1) * 2048]),
            "out_sh": np.ascontiguousarray(woutT[c * 1024:(c + 1) * 1024, :]),
        })

    res = run_bass_kernel_spmd(nc, in_maps, core_ids=list(range(8)))

    out = np.empty((4, 2048, 2048), np.float32)
    for c in range(8):
        b, h = c // 2, c % 2
        out[b, h * NT:(h + 1) * NT, :] = res.results[c]["yT"].T
    return out
